# version 03 (final)
# baseline (speedup 1.0000x reference)
import numpy as np
import ml_dtypes
import contextlib
import concourse.bass as bass
import concourse.mybir as mybir
from concourse.bass_utils import run_bass_kernel_spmd

F32 = mybir.dt.float32
BF16 = mybir.dt.bfloat16
BF = ml_dtypes.bfloat16
AF = mybir.ActivationFunctionType
ALU = mybir.AluOpType
AX = mybir.AxisListType.X

ENGINES = ("tensor", "vector", "scalar", "gpsimd", "sync")


class Prog:

    def __init__(self, nc):
        self.nc = nc
        self.ops = []
        self.sem_names = []
        self.counts = {}

    def sem(self, name):
        if name not in self.counts:
            self.counts[name] = 0
            self.sem_names.append(name)
        return name

    def val(self, name):
        return self.counts.get(name, 0)

    def emit(self, engine, fn, waits=(), inc=None, amt=None, tag=""):
        w = [(s, v) for (s, v) in waits]
        if inc is not None:
            self.sem(inc)
            if amt is None:
                amt = 16 if fn.__name__ == "dma" or "dma" in tag else 1
            self.counts[inc] += amt
        self.ops.append(dict(engine=engine, fn=fn, waits=w, inc=inc, amt=amt, tag=tag))
        return self.counts[inc] if inc is not None else None

    def dma(self, engine, out, in_, waits=(), inc=None, tag="dma", **kw):
        def fn(eng, out=out, in_=in_, kw=kw):
            return eng.dma_start(out, in_, **kw)
        fn.__name__ = "dma"
        return self.emit(engine, fn, waits, inc, amt=16, tag=tag)

    def dmat(self, engine, out, in_, waits=(), inc=None, tag="dmat"):
        def fn(eng, out=out, in_=in_):
            return eng.dma_start(out, in_, transpose=True)
        fn.__name__ = "dma"
        return self.emit(engine, fn, waits, inc, amt=16, tag=tag)

    def build(self, block_sems):
        per_engine = {e: [] for e in ENGINES}
        for op in self.ops:
            per_engine[op["engine"]].append(op)
        known = {e: {} for e in ENGINES}

        def emit_for(eng_name, eng):
            for op in per_engine[eng_name]:
                for (s, v) in op["waits"]:
                    if v <= 0:
                        continue
                    if known[eng_name].get(s, 0) >= v:
                        continue
                    eng.wait_ge(block_sems[s], v)
                    known[eng_name][s] = v
                inst = op["fn"](eng)
                if op["inc"] is not None:
                    inst.then_inc(block_sems[op["inc"]], op["amt"])
        return emit_for


def run_prog(nc, prog, final_waits):
    import contextlib
    with contextlib.ExitStack() as st:
        sems = {}
        for name in prog.sem_names:
            sems[name] = st.enter_context(nc.semaphore(name))
        block = st.enter_context(nc.Block())
        emit_for = prog.build(sems)

        @block.tensor
        def _(t):
            emit_for("tensor", t)

        @block.vector
        def _(v):
            emit_for("vector", v)

        @block.scalar
        def _(s):
            emit_for("scalar", s)

        @block.gpsimd
        def _(g):
            emit_for("gpsimd", g)

        @block.sync
        def _(s):
            emit_for("sync", s)
            for (name, v) in final_waits:
                s.wait_ge(sems[name], v)
    return nc




CSIZE = 64
NCH = 32
P2CUT = None


def build_fused(nchunks=NCH):
    NT = CSIZE * nchunks
    NJ = NT * 8
    NTQ = nchunks // 8
    nc = bass.Bass("TRN2", target_bir_lowering=False)
    X = nc.dram_tensor("x", [2 * NT, 2048], F32, kind="ExternalInput").ap()
    WT = {n: nc.dram_tensor(n, [2048, 512], F32, kind="ExternalInput").ap()
          for n in ("wq", "wk", "wv", "wa", "wg")}
    WBT4 = nc.dram_tensor("wb4", [2048, 4], F32, kind="ExternalInput").ap()
    BA4 = nc.dram_tensor("ba4", [128, 4], F32, kind="ExternalInput").ap()
    BB4 = nc.dram_tensor("bb4", [4, 1], F32, kind="ExternalInput").ap()
    CW = {n: nc.dram_tensor(f"cw{n}", [128, 4 * 4], F32, kind="ExternalInput").ap()
          for n in ("q", "k", "v")}
    CB = {n: nc.dram_tensor(f"cb{n}", [128, 4], F32, kind="ExternalInput").ap()
          for n in ("q", "k", "v")}
    NMASK0 = nc.dram_tensor("nmask", [128, 4 * 512], BF16, kind="ExternalInput").ap()
    PMASK0 = nc.dram_tensor("pmask", [128, 4 * 512], BF16, kind="ExternalInput").ap()
    SELC0 = nc.dram_tensor("selc", [128, 16 * 128], BF16, kind="ExternalInput").ap()
    SEL80 = nc.dram_tensor("sel8", [8, 128], BF16, kind="ExternalInput").ap()
    SELB0 = nc.dram_tensor("selb", [4, 32 * 128], BF16, kind="ExternalInput").ap()
    IDCB0 = nc.dram_tensor("idcb", [128, 128], BF16, kind="ExternalInput").ap()
    IDCF0 = nc.dram_tensor("idcf", [128, 128], F32, kind="ExternalInput").ap()
    LNGB0 = nc.dram_tensor("lngb", [128, 128], F32, kind="ExternalInput").ap()
    LNBB0 = nc.dram_tensor("lnbb", [128, 128], F32, kind="ExternalInput").ap()
    BM0 = nc.dram_tensor("bm", [128, 16], F32, kind="ExternalInput").ap()
    WOT0 = nc.dram_tensor("wot", [2048, 2048], F32, kind="ExternalInput").ap()
    AT0 = nc.dram_tensor("at", [2048, 512], F32, kind="ExternalInput").ap()
    BWT0 = nc.dram_tensor("bwt", [512, 2048], F32, kind="ExternalInput").ap()
    WGF0 = nc.dram_tensor("wgf", [2048, 8], F32, kind="ExternalInput").ap()
    X3 = nc.dram_tensor("x3", [NT // 2, 2048], F32, kind="ExternalInput").ap()
    OUTX = nc.dram_tensor("outx", [NT // 2, 2048], F32, kind="ExternalOutput").ap()
    QFD = nc.dram_tensor("qfd", [128, NJ], BF16, kind="Internal").ap()
    KFD = nc.dram_tensor("kfd", [128, NJ], BF16, kind="Internal").ap()
    VFD = nc.dram_tensor("vfd", [128, NJ], BF16, kind="Internal").ap()
    GFD = nc.dram_tensor("gfd", [128, NJ], BF16, kind="Internal").ap()
    ZFD = nc.dram_tensor("zfd", [128, NJ], F32, kind="Internal").ap()
    OFD = nc.dram_tensor("ofd", [128, 8 * NT], BF16, kind="Internal").ap()
    OGD = nc.dram_tensor("ogd", [4 * 128, 8 * NT], BF16, kind="Internal").ap()
    QOFF = nc.dram_tensor("qoff", [1, 4], mybir.dt.int32, kind="ExternalInput").ap()
    PMX0 = nc.dram_tensor("pmx", [128, 128], BF16, kind="ExternalInput").ap()

    p = Prog(nc)
    big = contextlib.ExitStack()
    sbB = lambda name, shape, dt: big.enter_context(nc.sbuf_tensor(name, shape, dt))
    psB = lambda name, shape: big.enter_context(nc.psum_tensor(name, shape, F32))
    PB = [psB(f"pbank{q}", [128, 512]) for q in range(4)]
    PS1 = psB("ps1", [128, 512])
    PS2 = psB("ps2", [128, 512])
    PS3 = psB("ps3", [128, 128])
    PS4 = psB("ps4", [128, 128])
    IDCB = sbB("idcbS", [128, 128], BF16)
    IDCF = sbB("idcfS", [128, 128], F32)
    ph12 = contextlib.ExitStack()
    sb12 = lambda name, shape, dt: ph12.enter_context(nc.sbuf_tensor(name, shape, dt))
    LNGB = sb12("lngbS", [128, 128], F32)
    LNBB = sb12("lnbbS", [128, 128], F32)
    BET = sb12("bet", [128, NT], F32)
    BMK = sb12("bmk", [128, 16], F32)
    BFMb = [sb12(f"bfm{b}", [4, NT], BF16) for b in range(2)]
    SELB = sb12("selbS", [4, 32 * 128], BF16)

    for (dst, src) in ((IDCB, IDCB0), (IDCF, IDCF0), (LNGB, LNGB0), (LNBB, LNBB0),
                       (BMK, BM0), (SELB, SELB0)):
        p.dma("gpsimd", dst[:], src[:], inc="cld")
    CLD0 = p.val("cld")

    ph1 = contextlib.ExitStack()
    sb1 = lambda name, shape, dt: ph1.enter_context(nc.sbuf_tensor(name, shape, dt))
    WS = {n: sb1(f"ws_{n}", [128, 16 * 512], BF16) for n in ("wq", "wk", "wv", "wa", "wg")}
    WB4s = sb1("wb4s", [128, 16 * 4], BF16)
    CWs = {n: sb1(f"cws{n}", [128, 16], F32) for n in ("q", "k", "v")}
    CBs = {n: sb1(f"cbs{n}", [128, 4], F32) for n in ("q", "k", "v")}
    BA4s = sb1("ba4s", [128, 4], F32)
    BB4s = sb1("bb4s", [4, 1], F32)
    XT1 = [sb1(f"xt1_{i}", [128, 2048], BF16) for i in range(2)]
    SQ1 = sb1("sq1", [128, 2048], BF16)
    ST1 = sb1("st1", [128, 8], F32)
    XBF = [sb1(f"xbf{i}", [128, 16 * 512], BF16) for i in range(2)]
    YW1 = sb1("yw1", [128, 516], BF16)
    CVA1 = sb1("cva1", [128, 512], F32)
    TL1 = {n: sb1(f"tl1{n}", [128, 8 * 4], BF16) for n in ("q", "k", "v")}
    QFV = sb1("qfv", [128, 4096], BF16)
    KFV = sb1("kfv", [128, 4096], BF16)
    VFV = sb1("vfv", [128, 4096], BF16)
    GFV = sb1("gfv", [128, 4096], BF16)
    ZFV = sb1("zfv", [128, 4096], F32)
    OB1 = sb1("ob1", [128, 512], F32)

    for n in ("wq", "wk", "wv", "wa", "wg"):
        p.dma("gpsimd", WS[n][:].rearrange("dp (dc f) -> dp dc f", dc=16),
              WT[n].rearrange("(dc dp) f -> dp dc f", dp=128), inc="cld")
    p.dma("gpsimd", WB4s[:].rearrange("dp (dc f) -> dp dc f", dc=16),
          WBT4.rearrange("(dc dp) f -> dp dc f", dp=128), inc="cld")
    for n in ("q", "k", "v"):
        p.dma("gpsimd", CWs[n][:], CW[n][:], inc="cld")
        p.dma("gpsimd", CBs[n][:], CB[n][:], inc="cld")
    p.dma("gpsimd", BA4s[:], BA4[:], inc="cld")
    p.dma("gpsimd", BB4s[:], BB4[:], inc="cld")
    CLD = p.val("cld")

    projs1 = [("wq", "q", QFV), ("wk", "k", KFV), ("wv", "v", VFV)]
    for tq in range(NTQ):
        for b in range(2):
            ib = tq * 2 + b
            ibuf = ib % 2
            for i in range(4):
                trow = b * NT + tq * 512 + i * 128
                xv = p.dma("gpsimd", XT1[ibuf][:], X[trow:trow + 128, :], inc="xld1",
                           waits=[("xfm1", p.val("xfm1"))])
                p.emit("vector", lambda e, ibuf=ibuf: e.reduce_sum(ST1[:, 0:1], XT1[ibuf][:], axis=AX),
                       waits=[("xld1", xv)], inc="vst1")
                sqv = p.emit("scalar", lambda e, ibuf=ibuf: e.activation(SQ1[:], XT1[ibuf][:], AF.Square),
                             waits=[("xld1", xv), ("sqr1", p.val("sqr1"))], inc="sq1")
                p.emit("vector", lambda e: e.reduce_sum(ST1[:, 1:2], SQ1[:], axis=AX),
                       waits=[("sq1", sqv)], inc="sqr1")

                def stats_fn(e):
                    e.drain()
                    e.tensor_scalar(ST1[:, 2:3], ST1[:, 0:1], 1.0 / 2048, None, ALU.mult)
                    e.tensor_scalar(ST1[:, 3:4], ST1[:, 1:2], 1.0 / 2048, None, ALU.mult)
                    e.drain()
                    e.tensor_tensor(ST1[:, 4:5], ST1[:, 2:3], ST1[:, 2:3], ALU.mult)
                    e.drain()
                    e.tensor_tensor(ST1[:, 5:6], ST1[:, 3:4], ST1[:, 4:5], ALU.subtract)
                    e.drain()
                    e.tensor_scalar(ST1[:, 5:6], ST1[:, 5:6], 1e-5, None, ALU.add)
                    e.drain()
                    return e.reciprocal(ST1[:, 6:7], ST1[:, 5:6])
                vv = p.emit("vector", stats_fn, inc="vst1")
                rv = p.emit("scalar", lambda e: e.activation(ST1[:, 7:8], ST1[:, 6:7], AF.Sqrt),
                            waits=[("vst1", vv), ("xh1", p.val("xh1"))], inc="rsq1")

                def xh_fn(e, ibuf=ibuf):
                    e.drain()
                    return e.tensor_scalar(XT1[ibuf][:], XT1[ibuf][:], ST1[:, 2:3], ST1[:, 7:8],
                                           ALU.subtract, ALU.mult)
                hv = p.emit("vector", xh_fn, waits=[("rsq1", rv)], inc="xh1")
                out_ap = XBF[ibuf][:].rearrange("dp (dc t) -> dp dc t", dc=16)[:, :, i * 128:(i + 1) * 128]
                p.dmat("sync", out_ap, XT1[ibuf][:], waits=[("xh1", hv)], inc="xfm1")
            XFMV = p.val("xfm1")
            for (wn, cn, dstv) in projs1:
                for h in range(4):
                    bh = b * 4 + h
                    def mmfn(e, wn=wn, h=h, ibuf=ibuf):
                        r = None
                        for dc in range(16):
                            r = e.matmul(PS1[:], WS[wn][:, dc * 512 + h * 128: dc * 512 + (h + 1) * 128],
                                         XBF[ibuf][:, dc * 512:(dc + 1) * 512],
                                         start=(dc == 0), stop=(dc == 15))
                        return r
                    mv = p.emit("tensor", mmfn, waits=[("cld", CLD), ("xfm1", XFMV),
                                                       ("ev1", p.val("ev1"))], inc="mm1")
                    if tq == 0:
                        tw = p.emit("vector", lambda e: e.memset(YW1[:, 0:4], 0.0),
                                    waits=[("cv1", p.val("cv1"))], inc="tl1")
                    else:
                        tw = p.emit("vector", lambda e, cn=cn, bh=bh: e.tensor_copy(
                            YW1[:, 0:4], TL1[cn][:, bh * 4:(bh + 1) * 4]),
                            waits=[("cv1", p.val("cv1")), ("tls", p.val("tls"))], inc="tl1")
                    ev = p.emit("scalar", lambda e: e.activation(YW1[:, 4:516], PS1[:], AF.Copy),
                                waits=[("mm1", mv), ("tl1", p.val("tl1")), ("cv1", p.val("cv1"))],
                                inc="ev1")

                    def convfn(e, cn=cn, h=h, b=b):
                        cwc = CWs[cn]
                        e.tensor_scalar(CVA1[:], YW1[:, 1:513], cwc[:, h * 4: h * 4 + 1], None, ALU.mult)
                        for m in range(1, 4):
                            e.drain()
                            e.scalar_tensor_tensor(CVA1[:], YW1[:, 1 + m:513 + m],
                                                   cwc[:, h * 4 + m: h * 4 + m + 1],
                                                   CVA1[:], ALU.mult, ALU.add)
                        e.drain()
                        return e.tensor_copy(TL1[cn][:, (b * 4 + h) * 4:(b * 4 + h + 1) * 4],
                                             YW1[:, 512:516])
                    cvv = p.emit("vector", convfn, waits=[("ev1", ev), ("sl1", p.val("sl1"))], inc="cv1")
                    p.emit("vector", lambda e: e.nop(), inc="tls")
                    p.emit("scalar", lambda e, dstv=dstv, cn=cn, h=h, bh=bh: e.activation(
                        dstv[:, bh:bh + 4089:8].rearrange("p t -> p t"), CVA1[:], AF.Silu,
                        bias=CBs[cn][:, h:h + 1]),
                        waits=[("cv1", cvv), ("vst", max(0, p.val("vst") - 0))], inc="sl1")
            for (wn, dstv, mode) in (("wa", ZFV, "z"), ("wg", GFV, "g")):
                for h in range(4):
                    bh = b * 4 + h
                    def mmfn2(e, wn=wn, h=h, ibuf=ibuf):
                        r = None
                        for dc in range(16):
                            r = e.matmul(PS1[:], WS[wn][:, dc * 512 + h * 128: dc * 512 + (h + 1) * 128],
                                         XBF[ibuf][:, dc * 512:(dc + 1) * 512],
                                         start=(dc == 0), stop=(dc == 15))
                        return r
                    mv = p.emit("tensor", mmfn2, waits=[("cld", CLD), ("xfm1", XFMV),
                                                        ("ev1", p.val("ev1"))], inc="mm1")
                    if mode == "z":
                        ev = p.emit("vector", lambda e, h=h, bh=bh: e.tensor_scalar(
                            ZFV[:, bh:bh + 4089:8], PS1[:], BA4s[:, h:h + 1], None, ALU.add),
                            waits=[("mm1", mv), ("vst", p.val("vst"))], inc="ev1")
                    else:
                        ev = p.emit("scalar", lambda e, bh=bh: e.activation(
                            GFV[:, bh:bh + 4089:8], PS1[:], AF.Sigmoid),
                            waits=[("mm1", mv), ("vst", p.val("vst"))], inc="ev1")
            def mmb(e, ibuf=ibuf):
                r = None
                for dc in range(16):
                    r = e.matmul(PS2[0:4, 0:512], WB4s[:, dc * 4:(dc + 1) * 4],
                                 XBF[ibuf][:, dc * 512:(dc + 1) * 512],
                                 start=(dc == 0), stop=(dc == 15))
                return r
            bmv = p.emit("tensor", mmb, waits=[("cld", CLD), ("xfm1", XFMV),
                                               ("bev1", p.val("bev1"))], inc="bmm1")
            p.emit("vector", lambda e, b=b, tq=tq: e.tensor_scalar(
                BFMb[b][:, tq * 512:(tq + 1) * 512], PS2[0:4, 0:512], BB4s[:], None, ALU.add),
                waits=[("bmm1", bmv)], inc="bev1")
        VW = p.val("sl1")
        EW = p.val("ev1")
        for (dv, sv) in ((QFD, QFV), (KFD, KFV), (VFD, VFV), (GFD, GFV), (ZFD, ZFV)):
            p.dma("sync", dv[:, tq * 4096:(tq + 1) * 4096], sv[:],
                  waits=[("sl1", VW), ("ev1", EW)], inc="vwr")
        p.emit("sync", lambda e: e.nop(), waits=[("vwr", p.val("vwr"))], inc="vst")

    for ts in range(16):
        def selmm2(e, ts=ts):
            r = None
            for b in range(2):
                r = e.matmul(PS1[:, 0:NT // 16],
                             SELB[:, (b * 16 + ts) * 128:(b * 16 + ts + 1) * 128],
                             BFMb[b][:, ts::16], start=(b == 0), stop=(b == 1))
            return r
        sm = p.emit("tensor", selmm2,
                    waits=[("bev1", p.val("bev1")), ("blv", p.val("blv")),
                           ("ev1", p.val("ev1"))], inc="blm")
        p.emit("vector", lambda e, ts=ts: e.tensor_copy(BET[:, ts::16], PS1[:, 0:NT // 16]),
               waits=[("blm", sm)], inc="blv")
    p.emit("scalar", lambda e: e.activation(BET[:], BET[:], AF.Exp, scale=-1.0),
           waits=[("blv", p.val("blv"))], inc="bsig")
    nrep = NT // 16

    def bfin(e):
        e.drain()
        e.tensor_scalar(BET[:], BET[:], 1.0, None, ALU.add)
        e.drain()
        e.reciprocal(BET[:], BET[:])
        e.drain()
        return e.tensor_tensor(
            BET[:], BET[:],
            BMK[:].rearrange("p (one m) -> p one m", one=1).broadcast_to([128, nrep, 16]),
            ALU.mult)
    p.emit("vector", bfin, waits=[("bsig", 1)], inc="bsig")
    P1DONE = ([("vwr", p.val("vwr"))] if p.val("vwr") else []) + [("bsig", 2)]
    return nc, p, big, ph1, ph12, P1DONE, dict(
        QFD=QFD, KFD=KFD, VFD=VFD, GFD=GFD, ZFD=ZFD, OFD=OFD, OGD=OGD,
        NMASK0=NMASK0, PMASK0=PMASK0, SELC0=SELC0, SEL80=SEL80, PMX0=PMX0,
        X=X, X3=X3, QOFF=QOFF, WOT0=WOT0, AT0=AT0, BWT0=BWT0, WGF0=WGF0, OUTX=OUTX,
        PB=PB, PS1=PS1, PS2=PS2, PS3=PS3, PS4=PS4, NT=NT,
        IDCB=IDCB, IDCF=IDCF, LNGB=LNGB, LNBB=LNBB, BET=BET, CLD0=CLD0)


def emit_phase2(nc, p, E, nchunks, ph2, p1done):
    sb = lambda name, shape, dt: ph2.enter_context(nc.sbuf_tensor(name, shape, dt))
    QF0, KF0, VF0, ZF0, GF0 = E["QFD"], E["KFD"], E["VFD"], E["ZFD"], E["GFD"]
    OFD = E["OFD"]
    PB, PS1, PS2, PS3, PS4 = E["PB"], E["PS1"], E["PS2"], E["PS3"], E["PS4"]
    IDCB, IDCF, LNGB, LNBB, BET = E["IDCB"], E["IDCF"], E["LNGB"], E["LNBB"], E["BET"]

    NB = 3
    QFC = [sb(f"qfc{i}", [128, 512], BF16) for i in range(NB)]
    KFC = [sb(f"kfc{i}", [128, 512], BF16) for i in range(NB)]
    VFC = [sb(f"vfc{i}", [128, 512], BF16) for i in range(NB)]
    ZFC = [sb(f"zfc{i}", [128, 512], F32) for i in range(NB)]
    GFC = [sb(f"gfc{i}", [128, 512], BF16) for i in range(NB)]
    GTC = [sb(f"gtc{i}", [128, 512], BF16) for i in range(NB)]
    NMASK = sb("nmaskS", [128, 4 * 512], BF16)
    PMASK = sb("pmaskS", [128, 4 * 512], BF16)
    SELC = sb("selcS", [128, 16 * 128], BF16)
    SEL8 = sb("sel8S", [8, 128], BF16)
    Z64 = sb("z64", [128, 64], F32)
    SPF = sb("spf", [128, 520], F32)
    E1F = sb("e1f", [128, 512], F32)
    NM = sb("nm", [128, 8], F32)
    EMC = sb("emc", [128, 8], F32)
    NEM = sb("nem", [128, 8], F32)
    LPF = sb("lpf", [128, 512], F32)
    PSIP = sb("psip", [128, 512], F32)
    LSF = sb("lsf", [128, 512], F32)
    PHIF = sb("phif", [128, 512], F32)
    VPF = sb("vpf", [128, 512], F32)
    XCF = sb("xcf", [128, 512], F32)
    SQF = sb("sqf", [128, 512], F32)
    EET = [sb(f"eet{q}", [128, 128], F32) for q in range(4)]
    PHT = [sb(f"pht{q}", [128, 128], F32) for q in range(4)]
    GV = [[sb(f"gv{b}_{q}", [128, 512], BF16) for q in range(4)] for b in range(2)]
    PV = [[sb(f"pv{b}_{q}", [128, 512], BF16) for q in range(4)] for b in range(2)]
    UB = [sb(f"ub{q}", [128, 128], BF16) for q in range(4)]
    UBX = sb("ubx", [128, 16 * 128], BF16)
    W2V = [sb(f"w2v{q}", [128, 128], BF16) for q in range(4)]
    KTV = sb("ktv", [128, 4 * 128], BF16)
    KTB = sb("ktb", [128, 4 * 128], BF16)
    W2B = sb("w2b", [128, 4 * 128], BF16)
    PCR = sb("pcr", [8, 128], BF16)
    DIAG = sb("diag", [128, 128], BF16)
    ZA = sb("za", [128, 8 * 128], BF16)
    SBst = sb("sbst", [128, 8 * 128], BF16)
    OPR = sb("opr", [128, 128], F32)
    OSQ = sb("osq", [128, 128], BF16)
    OST = sb("ost", [128, 8], F32)
    G1T = sb("g1t", [128, 128], F32)
    G2T = sb("g2t", [128, 128], F32)
    OF = [sb(f"of{q}", [128, 128], BF16) for q in range(4)]
    PMXS = sb("pmxs", [128, 128], BF16)
    BO = [sb(f"bo{i}", [128, 2048], BF16) for i in range(2)]

    for (dst, src) in ((NMASK, E["NMASK0"]), (PMASK, E["PMASK0"]),
                       (SELC, E["SELC0"]), (SEL8, E["SEL80"]), (PMXS, E["PMX0"])):
        p.dma("gpsimd", dst[:], src[:], inc="cld2", waits=p1done)
    p.emit("gpsimd", lambda e: e.memset(Z64[:], 0.0), inc="cmz2", amt=16, waits=p1done)
    p.emit("gpsimd", lambda e: e.memset(ZA[:], 0.0), inc="cmz2", amt=16)
    p.emit("gpsimd", lambda e: e.memset(SBst[:], 0.0), inc="cmz2", amt=16)
    p.emit("gpsimd", lambda e: e.memset(SPF[:, 0:8], 0.0), inc="cmz2", amt=16)
    p.emit("gpsimd", lambda e: e.memset(UBX[:], 0.0), inc="cmz2", amt=16)
    CLDALL = p.val("cld2")
    CMZALL = p.val("cmz2")

    for c in range(nchunks):
        b = c % NB
        b2 = c % 2
        J0 = c * 512
        wdep = [("ochunk", max(0, c - (NB - 1))), ("cld2", CLDALL), ("cmz2", CMZALL)]
        for dst, src in ((QFC[b], QF0), (KFC[b], KF0), (VFC[b], VF0), (GFC[b], GF0)):
            p.dma("gpsimd", dst[:], src[:, J0:J0 + 512], waits=wdep, inc=f"ld{b}")
        p.dma("gpsimd", ZFC[b][:], ZF0[:, J0:J0 + 512], waits=wdep, inc=f"ld{b}")
        LD = p.val(f"ld{b}")
        gtv = p.emit("sync", (lambda e, b=b: e.dma_start(
            GTC[b][:].rearrange("p (q v) -> p q v", q=4), GFC[b][:], transpose=True)),
            waits=[(f"ld{b}", LD), ("odone", max(0, c - 1) * 4)], inc=f"gtt{b}", amt=16, tag="dma")
        GTV = p.val(f"gtt{b}")

        if P2CUT is not None and P2CUT <= 1:
            p.emit("sync", lambda e: e.nop(), waits=[(f"ld{b}", LD)], inc="ochunk")
            continue
        p.emit("scalar", lambda e, b=b: e.activation(E1F[:], ZFC[b][:], AF.Exp, scale=-1.0),
               waits=[(f"ld{b}", LD), ("cld2", CLDALL), ("dk", p.val("dk")), ("fin", c * 64),
                      ("spf", p.val("spf"))], inc="e1f")
        sp1 = p.emit("scalar", lambda e: e.activation(SPF[:, 8:520], E1F[:], AF.Ln, bias=1.0),
                     waits=[("e1f", p.val("e1f"))], inc="spf")

        def nm_fn(e):
            r = None
            for bh in range(8):
                r = e.reduce_sum(NM[:, bh:bh + 1], SPF[:, 8 + bh: 8 + bh + 32 * 8: 8],
                                 axis=AX, negate=True)
            return r
        nmv = p.emit("vector", nm_fn, waits=[("spf", p.val("spf")), ("dke", p.val("dke"))], inc="dk")
        emv = p.emit("scalar", lambda e: e.activation(EMC[:], NM[:], AF.Exp),
                     waits=[("dk", nmv), ("dgb", p.val("dgb")), ("ps2e", p.val("ps2e"))], inc="dke")
        nemv = p.emit("vector", lambda e: e.tensor_scalar(NEM[:], EMC[:], -(128 ** -0.5), None, ALU.mult),
                      waits=[("dke", emv)], inc="dk")

        def scan_fn(e):
            e.drain()
            r = None
            for bh in range(8):
                r = e.tensor_tensor_scan(LPF[:, bh:bh + 505:8], SPF[:, bh:bh + 505:8],
                                         Z64[:], NM[:, bh:bh + 1], ALU.add, ALU.add)
            return r
        scv = p.emit("vector", scan_fn, waits=[("dk", nmv), ("dke", p.val("dke"))], inc="dk")
        psv = p.emit("scalar", lambda e: e.activation(PSIP[:], LPF[:], AF.Exp),
                     waits=[("dk", scv)], inc="dke")
        lsv = p.emit("vector", lambda e: e.tensor_tensor(LSF[:], LPF[:], SPF[:, 8:520], ALU.add),
                     waits=[("dk", scv), ("dke", p.val("dke"))], inc="dk")
        phv = p.emit("scalar", lambda e: e.activation(PHIF[:], LSF[:], AF.Exp, scale=-1.0),
                     waits=[("dk", lsv), ("ps1m", p.val("ps1m")), ("dgb", p.val("dgb")),
                            ("ps3m", p.val("ps3m"))], inc="dke")
        vpv = p.emit("vector", lambda e, b=b: e.tensor_tensor(VPF[:], PSIP[:], VFC[b][:], ALU.mult),
                     waits=[("dke", psv), (f"ld{b}", LD)], inc="dk")

        if P2CUT is not None and P2CUT <= 2:
            p.emit("sync", lambda e: e.nop(), waits=[(f"ld{b}", LD)], inc="ochunk")
            continue
        for q in range(4):
            tv = p.emit("tensor", lambda e, b=b, q=q: e.transpose(
                PS1[:, 0:128], ZFC[b][:, q * 128:(q + 1) * 128], IDCF[:]),
                waits=[(f"ld{b}", LD), ("cld2", CLDALL), ("ps1e", p.val("ps1e"))], inc="ps1m")
            p.emit("scalar", lambda e, q=q: e.activation(EET[q][:], PS1[:, 0:128], AF.Exp, scale=-1.0),
                   waits=[("ps1m", tv), ("fin", c * 64)], inc="ps1e")

        def eet1(e):
            r = None
            for q in range(4):
                r = e.tensor_scalar(EET[q][:], EET[q][:], 1.0, None, ALU.add)
            return r
        eev = p.emit("vector", eet1, waits=[("ps1e", p.val("ps1e"))], inc="eet1")

        if P2CUT is not None and P2CUT <= 3:
            p.emit("sync", lambda e: e.nop(), waits=[(f"ld{b}", LD)], inc="ochunk")
            continue
        for q in range(4):
            gm = p.emit("tensor", lambda e, b=b, q=q: e.matmul(
                PS1[:], KFC[b][:, q * 128:(q + 1) * 128], KFC[b][:], start=True, stop=True),
                waits=[(f"ld{b}", LD), ("ps1e", p.val("ps1e"))], inc="ps1m")
            p.emit("vector", lambda e, b2=b2, q=q: e.tensor_tensor(
                GV[b2][q][:], PS1[:], NMASK[:, q * 512:(q + 1) * 512], ALU.mult),
                waits=[("ps1m", gm), ("fin", max(0, c - 1) * 64)], inc="ps1e")
            pm = p.emit("tensor", lambda e, b=b, q=q: e.matmul(
                PS1[:], KFC[b][:, q * 128:(q + 1) * 128], QFC[b][:], start=True, stop=True),
                waits=[("ps1e", p.val("ps1e"))], inc="ps1m")
            p.emit("vector", lambda e, b2=b2, q=q: e.tensor_tensor(
                PV[b2][q][:], PS1[:], PMASK[:, q * 512:(q + 1) * 512], ALU.mult),
                waits=[("ps1m", pm), ("ogth", max(0, c - 1) * 10)], inc="ps1e")

        if P2CUT is not None and P2CUT <= 4:
            p.emit("sync", lambda e: e.nop(), waits=[(f"ld{b}", LD)], inc="ochunk")
            continue
        for bh in range(8):
            p.emit("tensor", lambda e, b=b, bh=bh: e.matmul(
                PS2[:, bh * 64:(bh + 1) * 64], ZA[:, bh * 128:(bh + 1) * 128],
                KFC[b][:, bh:bh + 505:8], start=True, stop=True),
                waits=[(f"ld{b}", LD), ("ps2e", p.val("ps2e")), ("sae", c * 16)], inc="ps2m")
        w0v = p.val("ps2m")

        def xc_fn(e):
            e.drain()
            r = None
            for bh in range(8):
                r = e.scalar_tensor_tensor(
                    XCF[:, bh:bh + 505:8], PS2[:, bh * 64:(bh + 1) * 64], NEM[:, bh:bh + 1],
                    VPF[:, bh:bh + 505:8], ALU.mult, ALU.add)
            return r
        xcv = p.emit("vector", xc_fn, waits=[("ps2m", w0v), ("dk", vpv),
                                             ("pinit", c * 4)], inc="ps2e")
        xcv2 = p.emit("vector", lambda e: e.engine_nop(), inc="xcv")

        if P2CUT is not None and P2CUT <= 5:
            p.emit("sync", lambda e: e.nop(), waits=[(f"ld{b}", LD)], inc="ochunk")
            continue
        for bh in range(8):
            p.emit("tensor", lambda e, b=b, bh=bh: e.matmul(
                PS2[:, bh * 64:(bh + 1) * 64], ZA[:, bh * 128:(bh + 1) * 128],
                QFC[b][:, bh:bh + 505:8], start=True, stop=True),
                waits=[("ps2e", p.val("ps2e"))], inc="ps2m")
        sqmv = p.val("ps2m")

        def sq_fn(e):
            r = None
            for bh in range(8):
                r = e.tensor_scalar(SQF[:, bh:bh + 505:8], PS2[:, bh * 64:(bh + 1) * 64],
                                    EMC[:, bh:bh + 1], None, ALU.mult)
            return r
        sqev = p.emit("vector", sq_fn, waits=[("ps2m", sqmv), ("dke", emv),
                                              ("oinit", c * 4)], inc="ps2e")

        if P2CUT is not None and P2CUT <= 6:
            p.emit("sync", lambda e: e.nop(), waits=[(f"ld{b}", LD)], inc="ochunk")
            continue
        for q in range(4):
            itv = p.emit("tensor", lambda e, q=q: e.matmul(
                PB[q][:, 0:128], XCF[:, q * 128:(q + 1) * 128], IDCF[:],
                is_transpose=True, start=True, stop=True, skip_group_check=True),
                waits=[("xcv", xcv2), ("fin", c * 64), ("oevq", c * 4)], inc="pinit")
            p.emit("vector", lambda e, q=q: e.memset(UB[q][:], 0.0),
                   waits=[("w2v", max(0, c - 1) * 4), ("ogth", max(0, c - 1) * 10),
                          ("gth", p.val("gth"))], inc="ucl")
            for q2 in range(q):
                p.emit("tensor", lambda e, b2=b2, q=q, q2=q2: e.matmul(
                    PB[q][:, 0:128], GV[b2][q2][:, q * 128:(q + 1) * 128], UB[q2][:],
                    start=False, stop=False, skip_group_check=True),
                    waits=[("ucp", c * 64 + (q2 + 1) * 16)], inc="gth")
            for ts in range(16):
                a64 = (ts // 8) * 64
                tglob = c * 64 + q * 16 + ts
                fv = p.emit("vector", lambda e, q=q, ts=ts, a64=a64, tglob=tglob: e.scalar_tensor_tensor(
                    UBX[a64:a64 + 64, ts * 128:(ts + 1) * 128],
                    PB[q][a64:a64 + 64, 0:128],
                    BET[a64:a64 + 64, tglob: tglob + 1],
                    EET[q][a64:a64 + 64, :], ALU.mult, ALU.mult),
                    waits=[("pinit", p.val("pinit")), ("gth", p.val("gth")),
                           ("sct", p.val("sct")), ("eet1", eev), ("bsig", 2),
                           ("w2v", max(0, c - 1) * 4), ("ogth", max(0, c - 1) * 10),
                           ("ucp", max(0, c * 64 + (q - 3) * 16)), ("ucl", p.val("ucl"))],
                    inc="fin")
                if ts >= 1:
                    pa = ((ts - 1) // 8) * 64
                    p.emit("vector", lambda e, q=q, ts=ts, pa=pa: e.tensor_tensor(
                        UB[q][pa:pa + 64, :], UB[q][pa:pa + 64, :],
                        UBX[pa:pa + 64, (ts - 1) * 128:ts * 128], ALU.add),
                        waits=[("ucp", p.val("ucp"))], inc="ucp")
                if ts < 15:
                    p.emit("tensor", lambda e, b2=b2, q=q, ts=ts: e.matmul(
                        PB[q][:, 0:128], GV[b2][q][:, q * 128:(q + 1) * 128],
                        UBX[:, ts * 128:(ts + 1) * 128], start=False, stop=False,
                        skip_group_check=True),
                        waits=[("fin", p.val("fin"))], inc="sct")

            def lastcp(e, q=q):
                e.drain()
                return e.tensor_tensor(UB[q][64:128, :], UB[q][64:128, :],
                                       UBX[64:128, 15 * 128:16 * 128], ALU.add)
            p.emit("vector", lastcp, inc="ucp")
        FIN = p.val("fin")

        if P2CUT is not None and P2CUT <= 7:
            p.emit("sync", lambda e: e.nop(), waits=[(f"ld{b}", LD)], inc="ochunk")
            continue
        ktv = p.emit("sync", (lambda e, b=b: e.dma_start(
            KTV[:].rearrange("p (q d) -> p q d", q=4), KFC[b][:], transpose=True)),
            waits=[(f"ld{b}", LD), ("selk", c * 16)], inc="kttr", amt=16, tag="dma")

        if P2CUT is not None and P2CUT <= 8:
            p.emit("sync", lambda e: e.nop(), waits=[(f"ld{b}", LD)], inc="ochunk")
            continue
        ptv = p.emit("tensor", lambda e: e.transpose(PS3[0:8, :], PHIF[:, 504:512], IDCF[:]),
                     waits=[("dke", phv), ("ps3e", p.val("ps3e"))], inc="ps3m")
        pev = p.emit("scalar", lambda e: e.activation(PCR[:], PS3[0:8, :], AF.Copy),
                     waits=[("ps3m", ptv), ("bcmm", p.val("bcmm"))], inc="ps3e")
        bcv = p.emit("tensor", lambda e: e.matmul(PS4[:], SEL8[:], PCR[:], start=True, stop=True),
                     waits=[("ps3e", pev), ("ps4e", p.val("ps4e"))], inc="bcmm")

        def w2_fn(e):
            e.drain()
            r = None
            for q in range(4):
                r = e.tensor_tensor(W2V[q][:], UB[q][:], PS4[:], ALU.mult)
            return r
        w2vv = p.emit("vector", w2_fn, waits=[("bcmm", bcv), ("ucp", (c + 1) * 64),
                                              ("selw", c * 16)], inc="w2v", amt=4)

        if P2CUT is not None and P2CUT <= 9:
            p.emit("sync", lambda e: e.nop(), waits=[(f"ld{b}", LD)], inc="ochunk")
            continue
        for (dst, src_tiles, sem_mm, wname, wval) in (
                (KTB, [KTV[:, qj * 128:(qj + 1) * 128] for qj in range(4)], "selk", "kttr", p.val("kttr")),
                (W2B, [W2V[qj][:] for qj in range(4)], "selw", "w2v", p.val("w2v"))):
            for q2 in range(4):
                def selmm(e, src_tiles=src_tiles, q2=q2):
                    r = None
                    for qj in range(4):
                        r = e.matmul(PS2[:, 0:128], SELC[:, (qj * 4 + q2) * 128:(qj * 4 + q2 + 1) * 128],
                                     src_tiles[qj], start=(qj == 0), stop=(qj == 3))
                    return r
                mmv = p.emit("tensor", selmm, waits=[(wname, wval), ("ps2e", p.val("ps2e"))],
                             inc="ps2m")
                p.emit("scalar", lambda e, dst=dst, q2=q2: e.activation(
                    dst[:, q2 * 128:(q2 + 1) * 128], PS2[:, 0:128], AF.Copy),
                    waits=[("ps2m", mmv), ("sav", c * 32)], inc="ps2e")
                p.emit("scalar", lambda e: e.nop(), inc=sem_mm, amt=4)
        EVK = p.val("ps2e")

        if P2CUT is not None and P2CUT <= 10:
            p.emit("sync", lambda e: e.nop(), waits=[(f"ld{b}", LD)], inc="ochunk")
            continue
        for bh in range(8):
            dgv = p.emit("vector", lambda e, bh=bh: e.tensor_scalar(
                DIAG[:], IDCB[:], PHIF[:, 504 + bh:505 + bh], EMC[:, bh:bh + 1], ALU.mult, ALU.mult),
                waits=[("dke", phv), ("sav", c * 32 + bh * 4)], inc="dgb")

            def sa_fn(e, bh=bh):
                half = (bh % 2) * 64
                pair = bh // 2
                kt = KTB[half:half + 64, pair * 128:(pair + 1) * 128]
                w2 = W2B[half:half + 64, pair * 128:(pair + 1) * 128]
                e.matmul(PS3[:], kt, w2, start=True, stop=False)
                e.matmul(PS3[:], SBst[:, bh * 128:(bh + 1) * 128], DIAG[:], start=False, stop=True)
                e.matmul(PS4[:], w2, kt, start=True, stop=False)
                return e.matmul(PS4[:], DIAG[:], SBst[:, bh * 128:(bh + 1) * 128], start=False, stop=True)
            sav = p.emit("tensor", sa_fn, waits=[("ps2e", EVK), ("dgb", p.val("dgb")),
                                                 ("ps3e", p.val("ps3e")), ("ps4e", p.val("ps4e")),
                                                 ("sae", c * 16 + bh * 2)],
                         inc="sav", amt=4)
            p.emit("scalar", lambda e, bh=bh: e.activation(ZA[:, bh * 128:(bh + 1) * 128], PS3[:], AF.Copy),
                   waits=[("sav", p.val("sav"))], inc="sae")
            p.emit("scalar", lambda e, bh=bh: e.activation(SBst[:, bh * 128:(bh + 1) * 128], PS4[:], AF.Copy),
                   waits=[("sav", p.val("sav"))], inc="sae")
            p.emit("scalar", lambda e: e.nop(), inc="ps3e")
            p.emit("scalar", lambda e: e.nop(), inc="ps4e")

        if P2CUT is not None and P2CUT <= 11:
            p.emit("sync", lambda e: e.nop(), waits=[(f"ld{b}", LD)], inc="ochunk")
            continue
        for q in range(4):
            p.emit("tensor", lambda e, q=q: e.matmul(
                PB[q][:, 0:128], SQF[:, q * 128:(q + 1) * 128], IDCF[:],
                is_transpose=True, start=True, stop=True, skip_group_check=True),
                waits=[("ps2e", sqev), ("fin", FIN), ("oevq", c * 4)], inc="oinit")
        OINIT = p.val("oinit")
        for q in range(4):
            for q2 in range(q + 1):
                p.emit("tensor", lambda e, b2=b2, q=q, q2=q2: e.matmul(
                    PB[q][:, 0:128], PV[b2][q2][:, q * 128:(q + 1) * 128], UB[q2][:],
                    start=False, stop=False, skip_group_check=True),
                    waits=[("oinit", OINIT), ("ucp", (c + 1) * 64)], inc="ogth")
        OGT = p.val("ogth")
        for q in range(4):
            tv = p.emit("tensor", lambda e, q=q: e.transpose(PS1[:, 0:128], PHIF[:, q * 128:(q + 1) * 128], IDCF[:]),
                        waits=[("dke", phv), ("ps1e", p.val("ps1e"))], inc="ps1m")
            p.emit("scalar", lambda e, q=q: e.activation(PHT[q][:], PS1[:, 0:128], AF.Copy),
                   waits=[("ps1m", tv), ("oevq", c * 4)], inc="ps1e")
        PHTD = p.val("ps1e")

        for q in range(4):
            def oev_fn(e, q=q):
                e.drain()
                e.tensor_tensor(OPR[:], PB[q][:, 0:128], PHT[q][:], ALU.mult)
                e.drain()
                return e.reduce_sum(OST[:, 0:1], OPR[:], axis=AX)
            ov = p.emit("vector", oev_fn, waits=[("ogth", OGT), ("ps1e", PHTD),
                                                 ("odone", c * 4 + q)], inc="oevq")
            sv2 = p.emit("vector", lambda e: e.tensor_tensor(OSQ[:], OPR[:], OPR[:], ALU.mult),
                         waits=[("oevq", ov)], inc="osq")

            def ostat(e):
                e.drain()
                e.reduce_sum(OST[:, 1:2], OSQ[:], axis=AX)
                e.drain()
                e.tensor_scalar(OST[:, 2:3], OST[:, 0:1], 1.0 / 128, None, ALU.mult)
                e.tensor_scalar(OST[:, 3:4], OST[:, 1:2], 1.0 / 128, None, ALU.mult)
                e.drain()
                e.tensor_tensor(OST[:, 4:5], OST[:, 2:3], OST[:, 2:3], ALU.mult)
                e.drain()
                e.tensor_tensor(OST[:, 5:6], OST[:, 3:4], OST[:, 4:5], ALU.subtract)
                e.drain()
                return e.tensor_scalar(OST[:, 5:6], OST[:, 5:6], 1e-5, None, ALU.add)
            ov2 = p.emit("vector", ostat, waits=[("osq", sv2)], inc="oev2")
            rv0 = p.emit("scalar", lambda e: e.activation(OST[:, 6:7], OST[:, 5:6], AF.Ln),
                         waits=[("oev2", ov2)], inc="orsq")
            rv = p.emit("scalar", lambda e: e.activation(OST[:, 7:8], OST[:, 6:7], AF.Exp, scale=-0.5),
                        waits=[("orsq", rv0)], inc="orsq")

            def ofin(e, q=q, b=b):
                e.drain()
                e.tensor_scalar(OPR[:], OPR[:], OST[:, 2:3], OST[:, 7:8], ALU.subtract, ALU.mult)
                e.tensor_tensor(G1T[:], LNGB[:], GTC[b][:, q * 128:(q + 1) * 128], ALU.mult)
                e.tensor_tensor(G2T[:], LNBB[:], GTC[b][:, q * 128:(q + 1) * 128], ALU.mult)
                e.drain()
                e.tensor_tensor(OPR[:], OPR[:], G1T[:], ALU.mult)
                e.drain()
                return e.tensor_tensor(OF[q][:], OPR[:], G2T[:], ALU.add)
            ov3 = p.emit("vector", ofin, waits=[("orsq", rv), ("opm", max(0, (c - 1) * 4 + q + 1)),
                                                (f"gtt{b}", GTV)], inc="odone")
            n = c * 4 + q
            g = c // 4
            slot = n % 2
            if P2CUT is not None and P2CUT <= 12:
                continue
            p.emit("tensor", (lambda e, q=q, slot=slot: e.matmul(
                PB[0][:, 256 + slot * 128:384 + slot * 128], OF[q][:], PMXS[:],
                start=True, stop=True)),
                waits=[("odone", p.val("odone")), ("opev", max(0, n - 1))], inc="opm")

            if P2CUT is not None and P2CUT <= 13:
                continue

            def oev(e, q=q, c=c, g=g, slot=slot):
                base = (c % 4) * 64 + q * 16
                return e.activation(
                    BO[g % 2][:].rearrange("v (bh t) -> v bh t", bh=8)[:, :, base:base + 16],
                    PB[0][:, 256 + slot * 128:384 + slot * 128].rearrange(
                        "v (bh t) -> v bh t", t=16),
                    AF.Copy)
            p.emit("scalar", oev, waits=[("opm", p.val("opm")),
                                         (f"ofw{g % 2}", p.val(f"ofw{g % 2}"))], inc="opev")
        if c % 4 == 3 and (P2CUT is None or P2CUT > 14):
            g = c // 4
            def odma(e, g=g):
                return e.dma_start(
                    OFD.rearrange("v (bh t) -> v bh t", bh=8)[:, :, g * 256:(g + 1) * 256],
                    BO[g % 2][:].rearrange("v (bh t) -> v bh t", bh=8))
            p.emit("sync", odma, waits=[("opev", (g + 1) * 16)], inc=f"ofw{g % 2}",
                   tag="dma", amt=16)
        p.emit("sync", lambda e: e.nop(), waits=[("opev", (c + 1) * 4), ("sae", (c + 1) * 16)],
               inc="ochunk")
    if P2CUT is not None:
        return [(n, v) for n, v in p.counts.items() if v > 0]
    return [(n, p.val(n)) for n in ("ochunk", "ofw0", "ofw1") if p.val(n) > 0]



def emit_phase3(nc, p, E, ph3, p2done):
    sb = lambda name, shape, dt: ph3.enter_context(nc.sbuf_tensor(name, shape, dt))
    X3, OUTX = E["X3"], E["OUTX"]
    OFD, OGD = E["OFD"], E["OGD"]
    PS1, PS2, PS3, PS4 = E["PS1"], E["PS2"], E["PS3"], E["PS4"]
    PB = E["PB"]
    PA = [PS1, PS2]
    PL = PB[0][:, 0:8]
    PHm = [PB[1], PB[2], PB[3], PS3]
    WOT0, AT0, BWT0, WGF0 = E["WOT0"], E["AT0"], E["BWT0"], E["WGF0"]
    NT = E["NT"]
    NQ = NT // 2
    NI = NQ // 128

    OTs = sb("ots3", [128, 16 * NQ], BF16)
    WOs = sb("wos3", [128, 16 * 2048], BF16)
    ATs = sb("ats3", [128, 16 * 512], BF16)
    BWs = sb("bws3", [128, 4 * 2048], BF16)
    WGs = sb("wgs3", [128, 16 * 8], BF16)
    XR = [sb(f"xr3{i}", [128, 2048], F32) for i in range(2)]
    X1 = [sb(f"x13{i}", [128, 2048], F32) for i in range(2)]
    XH2 = [sb(f"xh23{i}", [128, 2048], BF16) for i in range(2)]
    XF2 = [sb(f"xf23{i}", [128, 16 * 128], BF16) for i in range(2)]
    SQ = sb("sq3", [128, 2048], BF16)
    ST = sb("st3", [128, 8], F32)
    LG = sb("lg3", [128, 16], F32)
    EX8 = sb("ex83", [128, 8], F32)
    EQ8 = sb("eq83", [128, 8], F32)
    CS8 = sb("cs83", [128, 8], F32)
    FM8 = sb("fm83", [128, 8], F32)
    Z8 = sb("z83", [128, 8], F32)
    HM = sb("hm3", [128, 512], BF16)
    HMT = sb("hmt3", [128, 4 * 128], BF16)
    OE = [sb(f"oe3{i}", [128, 512], F32) for i in range(2)]

    ag = p.emit("gpsimd", lambda e: e.collective_compute(
        "AllGather", ALU.bypass, replica_groups=[[0, 1, 2, 3], [4, 5, 6, 7]],
        ins=[OFD.opt()], outs=[OGD.opt()]),
        waits=p2done, inc="ccd")
    CCD = p.val("ccd")

    QOS = sb("qos3", [1, 4], mybir.dt.int32)
    p.dma("gpsimd", QOS[:], E["QOFF"][:], inc="qosl")
    offcell = []
    for rk in range(4):
        for h in range(4):
            def otl(e, rk=rk, h=h):
                if not offcell:
                    for hh in range(4):
                        offcell.append(e.value_load(QOS[0:1, hh:hh + 1]))
                gh = rk * 4 + h
                return e.dma_start(OTs[:, gh * NQ:(gh + 1) * NQ],
                                   OGD[rk * 128:(rk + 1) * 128, bass.ds(offcell[h], NQ)])
            p.emit("gpsimd", otl, waits=[("ccd", CCD), ("qosl", 16)],
                   inc="wld3", tag="dma", amt=16)
    p.dma("gpsimd", WOs[:].rearrange("vp (vc d) -> vp vc d", vc=16),
          WOT0.rearrange("(vc vp) d -> vp vc d", vp=128), waits=[("ccd", CCD)], inc="wld3")
    p.dma("gpsimd", ATs[:].rearrange("dp (dc er) -> dp dc er", dc=16),
          AT0.rearrange("(dc dp) er -> dp dc er", dp=128), waits=[("ccd", CCD)], inc="wld3")
    p.dma("gpsimd", BWs[:].rearrange("ep (ec d) -> ep ec d", ec=4),
          BWT0.rearrange("(ec ep) d -> ep ec d", ep=128), waits=[("ccd", CCD)], inc="wld3")
    p.dma("gpsimd", WGs[:].rearrange("dp (dc e) -> dp dc e", dc=16),
          WGF0.rearrange("(dc dp) e -> dp dc e", dp=128), waits=[("ccd", CCD)], inc="wld3")
    p.emit("gpsimd", lambda e: e.memset(Z8[:], 0.0), waits=[("ccd", CCD)], inc="wmz3", amt=16)
    WLD3 = p.val("wld3")

    for i in range(NI):
        b = i % 2
        xv = p.dma("gpsimd", XR[b][:], X3[i * 128:(i + 1) * 128, :], inc=f"xld3{b}",
                   waits=[("rez", max(0, i - 1) * 4), ("ccd", CCD)])
        for db in range(4):
            bank = PA[db % 2]
            def mmo(e, i=i, db=db, bank=bank):
                r = None
                for vc in range(16):
                    r = e.matmul(bank[:], OTs[:, vc * NQ + i * 128: vc * NQ + (i + 1) * 128],
                                 WOs[:, vc * 2048 + db * 512:vc * 2048 + (db + 1) * 512],
                                 start=(vc == 0), stop=(vc == 15))
                return r
            gd = [("wld3", WLD3), ("rez", max(0, p.val("rez") - 1))]
            if db == 0:
                gd.append(("oev", max(0, (i - 1) * 4 + 3)))
            elif db == 1:
                gd.append(("oev", max(0, i * 4)))
            mv = p.emit("tensor", mmo, waits=gd, inc="mmo")
            p.emit("vector", lambda e, b=b, db=db, bank=bank: e.tensor_tensor(
                X1[b][:, db * 512:(db + 1) * 512], bank[:], XR[b][:, db * 512:(db + 1) * 512], ALU.add),
                waits=[("mmo", mv), (f"xld3{b}", xv), ("oev", max(0, (i - 1) * 4))], inc="rez")
        rz = p.val("rez")

        def red1(e, b=b):
            e.drain()
            return e.reduce_sum(ST[:, 0:1], X1[b][:], axis=AX)
        p.emit("vector", red1, waits=[("rez", rz)], inc="vst3")
        sqv = p.emit("scalar", lambda e, b=b: e.activation(SQ[:], X1[b][:], AF.Square),
                     waits=[("rez", rz), ("sqr3", p.val("sqr3"))], inc="sq3")
        p.emit("vector", lambda e: e.reduce_sum(ST[:, 1:2], SQ[:], axis=AX),
               waits=[("sq3", sqv)], inc="sqr3")

        def stats_fn(e):
            e.drain()
            e.tensor_scalar(ST[:, 2:3], ST[:, 0:1], 1.0 / 2048, None, ALU.mult)
            e.tensor_scalar(ST[:, 3:4], ST[:, 1:2], 1.0 / 2048, None, ALU.mult)
            e.drain()
            e.tensor_tensor(ST[:, 4:5], ST[:, 2:3], ST[:, 2:3], ALU.mult)
            e.drain()
            e.tensor_tensor(ST[:, 5:6], ST[:, 3:4], ST[:, 4:5], ALU.subtract)
            e.drain()
            e.tensor_scalar(ST[:, 5:6], ST[:, 5:6], 1e-5, None, ALU.add)
            e.drain()
            return e.reciprocal(ST[:, 6:7], ST[:, 5:6])
        vv = p.emit("vector", stats_fn, inc="vst3")
        rv = p.emit("scalar", lambda e: e.activation(ST[:, 7:8], ST[:, 6:7], AF.Sqrt),
                    waits=[("vst3", vv), ("xh3", p.val("xh3"))], inc="rsq3")

        def xh_fn(e, b=b):
            e.drain()
            return e.tensor_scalar(XH2[b][:], X1[b][:], ST[:, 2:3], ST[:, 7:8],
                                   ALU.subtract, ALU.mult)
        hv = p.emit("vector", xh_fn, waits=[("rsq3", rv), (f"xf3{b}", p.val(f"xf3{b}"))], inc="xh3")
        out_ap = XF2[b][:].rearrange("dp (dc t) -> dp dc t", dc=16)
        p.dmat("sync", out_ap, XH2[b][:], waits=[("xh3", hv)], inc=f"xf3{b}")
        xfv = p.val(f"xf3{b}")

        def mml(e, b=b):
            r = None
            for dc in range(16):
                r = e.matmul(PL, XF2[b][:, dc * 128:(dc + 1) * 128], WGs[:, dc * 8:(dc + 1) * 8],
                             start=(dc == 0), stop=(dc == 15))
            return r
        lmv = p.emit("tensor", mml, waits=[(f"xf3{b}", xfv), ("sel3", i * 10), ("wld3", WLD3),
                                           ("oevq", 0)], inc="mml")

        def sel_fn(e):
            e.tensor_copy(LG[:, 0:8], PL)
            e.drain()
            e.reduce_max(LG[:, 8:9], LG[:, 0:8], axis=AX)
            e.drain()
            e.tensor_scalar(LG[:, 9:10], LG[:, 8:9], -1.0, None, ALU.mult)
            e.drain()
            e.tensor_scalar(EQ8[:], LG[:, 0:8], LG[:, 8:9], None, ALU.is_equal)
            e.drain()
            e.tensor_tensor_scan(CS8[:], EQ8[:], Z8[:], 0.0, ALU.add, ALU.add)
            e.drain()
            e.tensor_scalar(FM8[:], CS8[:], 1.0, None, ALU.is_le)
            e.drain()
            return e.tensor_tensor(FM8[:], FM8[:], EQ8[:], ALU.mult)
        sv = p.emit("vector", sel_fn, waits=[("mml", lmv), ("wmz3", 16)], inc="sel3", amt=3)
        ev = p.emit("scalar", lambda e: e.activation(EX8[:], LG[:, 0:8], AF.Exp,
                                                     bias=LG[:, 9:10], scale=1.0,
                                                     accum_out=LG[:, 10:11]),
                    waits=[("sel3", p.val("sel3"))], inc="sel3", amt=3)

        def sc_fn(e):
            e.drain()
            e.reciprocal(LG[:, 11:12], LG[:, 10:11])
            e.drain()
            return e.tensor_scalar(LG[:, 12:13], LG[:, 11:12], 4.0, None, ALU.mult)
        scv = p.emit("vector", sc_fn, waits=[("sel3", p.val("sel3"))], inc="sel3", amt=4)

        def mmh(e, b=b):
            r = None
            for dc in range(16):
                for hb in range(4):
                    r = e.matmul(PHm[hb][:, 0:128], XF2[b][:, dc * 128:(dc + 1) * 128],
                                 ATs[:, dc * 512 + hb * 128:dc * 512 + (hb + 1) * 128],
                                 start=(dc == 0), stop=(dc == 15))
            return r
        hmv = p.emit("tensor", mmh, waits=[(f"xf3{b}", xfv), ("hmd3", i), ("wld3", WLD3),
                                           ("oevq", 0)], inc="mmh")

        def hv2_fn(e):
            e.drain()
            r = None
            for hb in range(4):
                r = e.scalar_tensor_tensor(
                    HM[:, hb * 128:(hb + 1) * 128].rearrange("p (e r) -> p e r", e=2),
                    PHm[hb][:, 0:128].rearrange("p (e r) -> p e r", e=2),
                    LG[:, 12:13],
                    FM8[:, hb * 2:hb * 2 + 2].rearrange("p (e one) -> p e one", one=1).broadcast_to([128, 2, 64]),
                    ALU.mult, ALU.mult)
            return r
        hv2 = p.emit("vector", hv2_fn,
                     waits=[("mmh", hmv), ("sel3", scv), ("hmt3", i * 16)], inc="hmd3")
        p.dmat("sync", HMT[:].rearrange("ep (ec t) -> ep ec t", ec=4), HM[:],
               waits=[("hmd3", hv2), ("mmy", i * 4)], inc="hmt3")

        for db in range(4):
            bank = PA[db % 2]
            def mmy(e, db=db, bank=bank):
                r = None
                for ec in range(4):
                    r = e.matmul(bank[:], HMT[:, ec * 128:(ec + 1) * 128],
                                 BWs[:, ec * 2048 + db * 512:ec * 2048 + (db + 1) * 512],
                                 start=(ec == 0), stop=(ec == 3))
                return r
            ymv = p.emit("tensor", mmy, waits=[("hmt3", p.val("hmt3")), ("oev", max(0, p.val("oev") - 1)),
                                               ("rez", (i + 1) * 4), ("wld3", WLD3)], inc="mmy")
            oev = p.emit("vector", lambda e, b=b, db=db, bank=bank: e.tensor_tensor(
                OE[db % 2][:], bank[:], X1[b][:, db * 512:(db + 1) * 512], ALU.add),
                waits=[("mmy", ymv), (f"oz3{db % 2}", p.val(f"oz3{db % 2}"))], inc="oev")
            p.dma("sync", OUTX[i * 128:(i + 1) * 128, db * 512:(db + 1) * 512], OE[db % 2][:],
                  waits=[("oev", p.val("oev"))], inc=f"oz3{db % 2}")
    return [("oz30", p.val("oz30")), ("oz31", p.val("oz31"))]


def _prune_waits(p):
    for op in p.ops:
        op["waits"] = [(s, v) for (s, v) in op["waits"] if v <= p.counts.get(s, 0)]


def build_all(nchunks=NCH, phases=3):
    nc, p, big, ph1, ph12, P1DONE, E = build_fused(nchunks)
    if phases == 1:
        run_prog(nc, p, P1DONE)
        ph1.close()
        ph12.close()
        big.close()
        return nc
    ph1.close()
    ph2 = contextlib.ExitStack()
    p2done = emit_phase2(nc, p, E, nchunks, ph2, P1DONE)
    if phases == 2:
        if P2CUT is not None:
            _prune_waits(p)
        run_prog(nc, p, p2done + P1DONE)
        ph2.close()
        ph12.close()
        big.close()
        return nc
    ph2.close()
    ph12.close()
    ph3 = contextlib.ExitStack()
    p3done = emit_phase3(nc, p, E, ph3, p2done)
    ph3.close()
    run_prog(nc, p, p3done + P1DONE)
    big.close()
    return nc



C1 = 128 ** -0.5
C2 = C1 * C1


def l2_consts():
    j = np.arange(512)
    t = j >> 3
    bh = j & 7
    nmask = np.zeros((512, 512), np.float32)
    pmask = np.zeros((512, 512), np.float32)
    same = bh[:, None] == bh[None, :]
    nmask[same & (t[:, None] < t[None, :])] = -C2
    pmask[same & (t[:, None] <= t[None, :])] = C1
    NM = np.zeros((128, 4 * 512), BF)
    PM = np.zeros((128, 4 * 512), BF)
    for qr in range(4):
        NM[:, qr * 512:(qr + 1) * 512] = nmask[qr * 128:(qr + 1) * 128, :].astype(BF)
        PM[:, qr * 512:(qr + 1) * 512] = pmask[qr * 128:(qr + 1) * 128, :].astype(BF)
    SELC = np.zeros((128, 16 * 128), BF)
    for qj in range(4):
        for q2 in range(4):
            tile = np.zeros((128, 128), np.float32)
            for jl in range(128):
                jg = qj * 128 + jl
                j2g = (jg & 7) * 64 + (jg >> 3)
                if q2 * 128 <= j2g < (q2 + 1) * 128:
                    tile[jl, j2g - q2 * 128] = 1.0
            SELC[:, (qj * 4 + q2) * 128:(qj * 4 + q2 + 1) * 128] = tile.astype(BF)
    SEL8 = np.zeros((8, 128), np.float32)
    for b in range(8):
        SEL8[b, np.arange(b, 128, 8)] = C1
    BM = np.zeros((128, 16), np.float32)
    for m in range(16):
        BM[m * 8:(m + 1) * 8, m] = 1.0
    selb = np.zeros((4, 32 * 128), np.float32)
    for b in range(2):
        for ts in range(16):
            for h in range(4):
                selb[h, (b * 16 + ts) * 128 + ts * 8 + b * 4 + h] = 1.0
    pmx = np.zeros((128, 128), np.float32)
    for tl in range(16):
        for bh in range(8):
            pmx[tl * 8 + bh, bh * 16 + tl] = 1.0
    return dict(nmask=np.ascontiguousarray(NM), pmask=np.ascontiguousarray(PM),
                selc=np.ascontiguousarray(SELC), sel8=np.ascontiguousarray(SEL8.astype(BF)),
                idcb=np.ascontiguousarray(np.eye(128, dtype=np.float32).astype(BF)),
                idcf=np.ascontiguousarray(np.eye(128, dtype=np.float32)),
                bm=np.ascontiguousarray(BM),
                selb=np.ascontiguousarray(selb.astype(BF)),
                pmx=np.ascontiguousarray(pmx.astype(BF)))


def build_shared(inputs):
    x = np.asarray(inputs["L0_attn_x"], np.float32)
    sh = {"xflat": np.ascontiguousarray(x.reshape(8192, 2048))}
    for nm in ("Wq", "Wk", "Wv", "Wa", "Wg"):
        sh[nm] = np.ascontiguousarray(np.asarray(inputs[f"L0_attn_{nm}"], np.float32).T)
    sh["WbT"] = np.ascontiguousarray(np.asarray(inputs["L0_attn_Wb"], np.float32).T)
    sh["ba"] = np.asarray(inputs["L0_attn_ba"], np.float32)
    sh["bb"] = np.asarray(inputs["L0_attn_bb"], np.float32)
    for cn in ("q", "k", "v"):
        sh[f"conv_{cn}_w"] = np.asarray(inputs[f"L0_attn_conv_{cn}_w"], np.float32)
        sh[f"conv_{cn}_b"] = np.asarray(inputs[f"L0_attn_conv_{cn}_b"], np.float32)
    ln_g = np.asarray(inputs["L0_attn_ln_g"], np.float32)
    ln_b = np.asarray(inputs["L0_attn_ln_b"], np.float32)
    sh["lngb"] = np.ascontiguousarray(np.broadcast_to(ln_g, (128, 128)).copy())
    sh["lnbb"] = np.ascontiguousarray(np.broadcast_to(ln_b, (128, 128)).copy())
    sh["wot"] = np.ascontiguousarray(np.asarray(inputs["L0_attn_Wo"], np.float32).T)
    A = np.asarray(inputs["L0_ffn_A"], np.float32)
    Bw = np.asarray(inputs["L0_ffn_Bw"], np.float32)
    sh["at"] = np.ascontiguousarray(A.transpose(2, 0, 1).reshape(2048, 512))
    sh["bwt"] = np.ascontiguousarray(Bw.transpose(0, 2, 1).reshape(512, 2048))
    sh["wgf"] = np.ascontiguousarray(np.asarray(inputs["L0_ffn_Wg"], np.float32).T)
    return sh


def host_core_inputs(core, l2c, sh):
    bp, hq = core // 4, core % 4
    xflat = sh["xflat"]
    d = dict(l2c)
    d["x"] = np.ascontiguousarray(xflat[bp * 4096:(bp + 1) * 4096])
    d["x3"] = np.ascontiguousarray(xflat[bp * 4096 + hq * 1024: bp * 4096 + hq * 1024 + 1024])
    for wn, nm in (("wq", "Wq"), ("wk", "Wk"), ("wv", "Wv"), ("wa", "Wa"), ("wg", "Wg")):
        d[wn] = np.ascontiguousarray(sh[nm][:, 4 * hq * 128:(4 * hq + 4) * 128])
    d["wb4"] = np.ascontiguousarray(sh["WbT"][:, 4 * hq:4 * hq + 4])
    ba4 = np.empty((128, 4), np.float32)
    for h in range(4):
        ba4[:, h] = sh["ba"][(4 * hq + h) * 128:(4 * hq + h + 1) * 128]
    d["ba4"] = ba4
    d["bb4"] = np.ascontiguousarray(sh["bb"][4 * hq:4 * hq + 4].reshape(4, 1))
    for cn in ("q", "k", "v"):
        cwf, cbf = sh[f"conv_{cn}_w"], sh[f"conv_{cn}_b"]
        cw = np.empty((128, 16), np.float32)
        cb = np.empty((128, 4), np.float32)
        for h in range(4):
            ch0 = (4 * hq + h) * 128
            for m in range(4):
                cw[:, h * 4 + m] = cwf[ch0:ch0 + 128, 0, m]
            cb[:, h] = cbf[ch0:ch0 + 128]
        d[f"cw{cn}"] = cw
        d[f"cb{cn}"] = cb
    for k in ("lngb", "lnbb", "wot", "at", "bwt", "wgf"):
        d[k] = sh[k]
    qoff = np.empty((1, 4), np.int32)
    for h in range(4):
        qoff[0, h] = ((hq >> 1) * 4 + h) * 2048 + (hq & 1) * 1024
    d["qoff"] = qoff
    return d


_NC = None
_L2C = None


def kernel(**inputs):
    global _NC, _L2C
    if _NC is None:
        _NC = build_all(32)
        _L2C = l2_consts()
    sh = build_shared(inputs)
    in_maps = [host_core_inputs(c, _L2C, sh) for c in range(8)]
    res = run_bass_kernel_spmd(_NC, in_maps, core_ids=list(range(8)))
    out = np.empty((8192, 2048), np.float32)
    for core in range(8):
        bp, hq = core // 4, core % 4
        out[bp * 4096 + hq * 1024: bp * 4096 + hq * 1024 + 1024] = \
            np.asarray(res.results[core]["outx"])
    return out.reshape(4, 2048, 2048)
